# version 04 (final)
# baseline (speedup 1.0000x reference)
import contextlib
import numpy as np
import concourse.bass as bass
import concourse.mybir as mybir
from concourse.bass_utils import run_bass_kernel_spmd

F32 = mybir.dt.float32
F16 = mybir.dt.float16
AF = mybir.ActivationFunctionType
ALU = mybir.AluOpType
AXX = mybir.AxisListType.X

D = 2048
NCH = 16
T = 2048
HALO = 3
WE = T + HALO
TILES = [(0, HALO)] + [(HALO + 512 * i, 512) for i in range(4)]
IFF = 8192
ICH = 64
CHUNK = 256
NSEG = T // CHUNK
GROUPS = [[0, 1, 2, 3], [4, 5, 6, 7]]
VKN = 64 * 32 * 32
STATS_ROWS = 64

MODE = "fused"
C3LVL = 9
PHASES = "ABCD"



def _ffn0_perm():
    n = np.arange(IFF)
    return 4 * (n % 2048) + (n // 2048)

def _pp(v):
    return np.ascontiguousarray(v.reshape(NCH, 128).T)

def host_prep(inputs):
    f = np.float32
    sh = {}
    perm = _ffn0_perm()
    sh["wqkvT"] = np.ascontiguousarray(inputs["L0_attn_w_qkv"].T)
    sh["wprojT"] = np.ascontiguousarray(inputs["L0_attn_w_proj"].T)
    sh["bproj_r"] = _pp(inputs["L0_attn_b_proj"])
    sh["wg0"] = np.ascontiguousarray(inputs["L0_ffn_w_gate"][:, perm])
    sh["wu0"] = np.ascontiguousarray(inputs["L0_ffn_w_up"][:, perm])
    sh["wd0"] = np.ascontiguousarray(inputs["L0_ffn_w_down"][perm, :])
    sh["nscale_r"] = _pp(inputs["L1_attn_norm_scale"])
    cw = inputs["L1_attn_conv_w"][:, 0, :]
    sh["convw_r"] = np.ascontiguousarray(
        cw.reshape(NCH, 128, 4).transpose(1, 0, 2).reshape(128, NCH * 4))
    sh["convb_r"] = _pp(inputs["L1_attn_conv_b"])
    sh["winT"] = np.ascontiguousarray(inputs["L1_attn_in_proj_w"].T)
    sh["dtb_r"] = _pp(inputs["L1_attn_dt_bias"])
    sh["woutT"] = np.ascontiguousarray(inputs["L1_attn_out_proj_w"].T)
    sh["wg1T"] = np.ascontiguousarray(inputs["L1_ffn_w_gate"].T)
    sh["wu1T"] = np.ascontiguousarray(inputs["L1_ffn_w_up"].T)
    sh["wd1T"] = np.ascontiguousarray(inputs["L1_ffn_w_down"].T)
    M = inputs["L1_attn_head_mix"]
    BM = np.zeros((D, D), f)
    ar = np.arange(128)
    for h in range(16):
        for m in range(16):
            BM[h * 128 + ar, m * 128 + ar] = M[h, m]
    sh["BM"] = BM
    ind = np.zeros((D, 64), f)
    ind[np.arange(D), np.arange(D) // 32] = 1.0
    sh["indden"] = ind
    sh["bbc"] = np.ascontiguousarray(ind.T)
    sh["ident"] = np.eye(128, dtype=f)
    tri = np.zeros((32, 32), f); tris = np.zeros((32, 32), f)
    jj, cc = np.meshgrid(np.arange(32), np.arange(32), indexing="ij")
    tri[jj <= cc] = 1.0
    tris[jj < cc] = 1.0
    sh["tri32"] = tri
    sh["tri32s"] = tris

    x = inputs["L0_attn_x"]
    cores = []
    for c in range(8):
        b, r = c // 4, c % 4
        pc = {}
        xe = np.zeros((D, WE), f)
        xe[:, HALO:] = x[b, 2048 * r:2048 * (r + 1), :].T
        if r > 0:
            xe[:, :HALO] = x[b, 2048 * r - HALO:2048 * r, :].T
        pc["x_ext"] = np.ascontiguousarray(xe)
        pos = np.arange(2048 * r, 2048 * (r + 1), dtype=f)
        inv = (1.0 / 10000.0 ** (np.arange(0, 128, 2, dtype=f) / np.float32(128.0))).astype(f)
        fr = inv[:, None] * pos[None, :]
        pc["cos_t"] = np.cos(fr).astype(f)
        pc["sin_t"] = np.sin(fr).astype(f)
        sp = np.zeros((16, STATS_ROWS), f)
        sp[np.arange(16), r * 16 + np.arange(16)] = 1.0
        pc["selplace"] = sp
        sc = np.zeros((32, 8), f)
        sc[8 * r + np.arange(8), np.arange(8)] = 1.0
        pc["selcarry"] = sc
        cores.append(pc)
    return sh, cores



class Sem:
    def __init__(self, nc, name):
        self.h = nc.alloc_semaphore(name)
        self.v = 0
    def inc(self, inst, n=1):
        inst.then_inc(self.h, n)
        self.v += n


class P:
    def __init__(self, nc):
        self.nc = nc
        self.gl = Sem(nc, "s_gl")
        self.sl = Sem(nc, "s_sl")
        self.st = Sem(nc, "s_st")
        self.g2 = Sem(nc, "s_g2")
        self.pe = Sem(nc, "s_pe")
        self.ac = Sem(nc, "s_ac")
        self.dv = Sem(nc, "s_dv")
        self.cc = Sem(nc, "s_cc")
    def bar(self):
        self.nc.all_engine_barrier()


def _w(eng, sem, v=None):
    v = sem.v if v is None else v
    if v > 0:
        eng.wait_ge(sem.h, v)


def wslice(wdram, m, mw=128):
    return wdram.rearrange("(k p) f -> p k f", p=128)[:, :, m * mw:(m + 1) * mw]


def dram_rows(dram, m, off, w):
    return dram[m * 128:(m + 1) * 128, off:off + w]


def dram_tiles(dram, off, w):
    return dram.rearrange("(c p) n -> p c n", p=128)[:, :, off:off + w]


def dram_tiles64(dram, off, w):
    return dram.rearrange("(c p) n -> p c n", p=128)[:, :, off:off + w]



def emit_proj(p, blk, stk, *, tag, Ms, nK, wap, rhs, tiles, evac, mw=128,
              force_nsets=None, pe_pre=None):
    nc = p.nc
    nt = len(tiles)
    nsets = force_nsets or max(1, min(2, 8 // nt))
    wts = [stk.enter_context(nc.sbuf_tensor(f"pw{i}_{tag}", [128, nK * mw], F16))
           for i in range(2)]
    pss = [[stk.enter_context(nc.psum_tensor(f"pp{s}_{i}_{tag}", [128, 512], F32))
            for i in range(nt)] for s in range(nsets)]
    gl_base = p.gl.v
    pe_base = p.pe.v
    ev_toks = []

    @blk.gpsimd
    def _(g):
        for mi, m in enumerate(Ms):
            if mi >= 2:
                g.wait_ge(p.pe.h, pe_base + mi - 1)
            inst = g.dma_start(out=wts[mi % 2][:], in_=wap(m))
            p.gl.inc(inst, 16)

    for mi, m in enumerate(Ms):
        @blk.tensor
        def _(t, mi=mi, m=m):
            if mi == 0 and pe_pre:
                for s_, v_ in pe_pre:
                    t.wait_ge(s_.h, v_)
            _w(t, p.gl, gl_base + 16 * (mi + 1))
            if mi >= nsets:
                s_, v_ = ev_toks[mi - nsets]
                t.wait_ge(s_.h, v_)
            ps = pss[mi % nsets]
            wt = wts[mi % 2]
            inst = None
            for k in range(nK):
                lhs = wt[:, k * mw:(k + 1) * mw]
                for ti, (off, w) in enumerate(tiles):
                    inst = t.matmul(ps[ti][0:128, 0:w], lhs, rhs(k, off, w),
                                    start=(k == 0), stop=(k == nK - 1),
                                    skip_group_check=True)
            p.pe.inc(inst)
        evac(m, mi, pss[mi % nsets], p.pe.v, ev_toks)
        assert len(ev_toks) == mi + 1, "evac must append exactly one token"



def alloc_ln(p, stk, tag):
    nc = p.nc
    ln = {}
    ln["x16"] = stk.enter_context(nc.sbuf_tensor(f"lnx_{tag}", [128, NCH * 512], F16))
    ln["sq16"] = stk.enter_context(nc.sbuf_tensor(f"lnq_{tag}", [128, NCH * 512], F16))
    ln["ones16"] = stk.enter_context(nc.sbuf_tensor(f"lno_{tag}", [128, 1], F16))
    ln["ones1f"] = stk.enter_context(nc.sbuf_tensor(f"ln1_{tag}", [1, 128], F32))
    ln["stat"] = stk.enter_context(nc.sbuf_tensor(f"lns_{tag}", [1, 2560], F32))
    ln["mb"] = stk.enter_context(nc.sbuf_tensor(f"lnm_{tag}", [128, 512], F32))
    ln["rb"] = stk.enter_context(nc.sbuf_tensor(f"lnr_{tag}", [128, 512], F32))
    ln["tmp"] = stk.enter_context(nc.sbuf_tensor(f"lnt_{tag}", [128, 512], F32))
    ln["ps_s"] = stk.enter_context(nc.psum_tensor(f"lnps_{tag}", [128, 512], F32))
    ln["ps_q"] = stk.enter_context(nc.psum_tensor(f"lnpq_{tag}", [128, 512], F32))
    ln["ps_m"] = stk.enter_context(nc.psum_tensor(f"lnpm_{tag}", [128, 512], F32))
    ln["ps_r"] = stk.enter_context(nc.psum_tensor(f"lnpr_{tag}", [128, 512], F32))
    return ln


def emit_ln_init(p, blk, ln):
    @blk.vector
    def _(v):
        v.memset(ln["ones16"][:], 1.0)
        inst = v.memset(ln["ones1f"][:], 1.0)
        p.dv.inc(inst)


def emit_ln_tile(p, blk, ln, srcap, dstap, w, eps=1e-5):
    x16, sq16 = ln["x16"], ln["sq16"]

    @blk.vector
    def _(v):
        _w(v, p.sl)
        _w(v, p.gl)
        for c in range(NCH):
            inst = v.tensor_copy(x16[:, c * 512:c * 512 + w], srcap(c))
        p.dv.inc(inst)
    dv_cast = p.dv.v

    @blk.scalar
    def _(s):
        _w(s, p.dv, dv_cast)
        for c in range(NCH):
            inst = s.activation(sq16[:, c * 512:c * 512 + w],
                                x16[:, c * 512:c * 512 + w], AF.Square)
        p.ac.inc(inst)
    ac_sq = p.ac.v

    @blk.tensor
    def _(t):
        _w(t, p.ac, ac_sq)
        for c in range(NCH):
            t.matmul(ln["ps_s"][0:1, 0:w], ln["ones16"][0:128, 0:1],
                     x16[:, c * 512:c * 512 + w],
                     start=(c == 0), stop=(c == NCH - 1), skip_group_check=True)
        for c in range(NCH):
            inst = t.matmul(ln["ps_q"][0:1, 0:w], ln["ones16"][0:128, 0:1],
                            sq16[:, c * 512:c * 512 + w],
                            start=(c == 0), stop=(c == NCH - 1), skip_group_check=True)
        p.pe.inc(inst)
    pe_stats = p.pe.v

    st = ln["stat"]

    @blk.scalar
    def _(s):
        _w(s, p.pe, pe_stats)
        s.activation(st[0:1, 0:w], ln["ps_s"][0:1, 0:w], AF.Copy, scale=1.0 / 2048.0)
        inst = s.activation(st[0:1, 512:512 + w], ln["ps_q"][0:1, 0:w], AF.Copy,
                            scale=1.0 / 2048.0)
        p.ac.inc(inst)
    ac_stat = p.ac.v

    @blk.vector
    def _(v):
        _w(v, p.ac, ac_stat)
        inst = v.tensor_mul(st[0:1, 1024:1024 + w], st[0:1, 0:w], st[0:1, 0:w])
        p.dv.inc(inst)
        v.wait_ge(p.dv.h, p.dv.v)
        inst = v.tensor_sub(st[0:1, 1536:1536 + w], st[0:1, 512:512 + w],
                            st[0:1, 1024:1024 + w])
        p.dv.inc(inst)
        v.wait_ge(p.dv.h, p.dv.v)
        inst = v.tensor_scalar_add(st[0:1, 1536:1536 + w], st[0:1, 1536:1536 + w],
                                   float(eps))
        p.dv.inc(inst)
        v.wait_ge(p.dv.h, p.dv.v)
        inst = v.reciprocal(st[0:1, 1024:1024 + w], st[0:1, 1536:1536 + w])
        p.dv.inc(inst)
    dv_var = p.dv.v

    @blk.scalar
    def _(s):
        _w(s, p.dv, dv_var)
        inst = s.activation(st[0:1, 2048:2048 + w], st[0:1, 1024:1024 + w], AF.Sqrt)
        p.ac.inc(inst)
    ac_rstd = p.ac.v

    @blk.tensor
    def _(t):
        _w(t, p.ac, ac_rstd)
        t.matmul(ln["ps_m"][0:128, 0:w], ln["ones1f"][0:1, 0:128], st[0:1, 0:w],
                 start=True, stop=True)
        inst = t.matmul(ln["ps_r"][0:128, 0:w], ln["ones1f"][0:1, 0:128],
                        st[0:1, 2048:2048 + w], start=True, stop=True)
        p.pe.inc(inst)
    pe_b = p.pe.v

    @blk.vector
    def _(v):
        _w(v, p.pe, pe_b)
        v.tensor_copy(ln["mb"][:, 0:w], ln["ps_m"][0:128, 0:w])
        inst = v.tensor_copy(ln["rb"][:, 0:w], ln["ps_r"][0:128, 0:w])
        p.dv.inc(inst)
        v.wait_ge(p.dv.h, p.dv.v)
        for c in range(NCH):
            inst = v.tensor_sub(ln["tmp"][:, 0:w], srcap(c), ln["mb"][:, 0:w])
            p.dv.inc(inst)
            v.wait_ge(p.dv.h, p.dv.v)
            inst = v.tensor_mul(dstap(c), ln["tmp"][:, 0:w], ln["rb"][:, 0:w])
            p.dv.inc(inst)
            v.wait_ge(p.dv.h, p.dv.v)
        p.dv.inc(v.engine_nop())



def phase_a(p, io):
    nc = p.nc
    with contextlib.ExitStack() as stk:
        xn016 = stk.enter_context(nc.sbuf_tensor("a_xn", [128, NCH * WE], F16))

        def xn_ap(c, off, w):
            return xn016[:, c * WE + off:c * WE + off + w]

        with contextlib.ExitStack() as s0:
            xf32 = s0.enter_context(nc.sbuf_tensor("a_xf", [128, NCH * 512], F32))
            ln = alloc_ln(p, s0, "a")
            with nc.Block() as blk:
                emit_ln_init(p, blk, ln)
            for (off, w) in TILES:
                with nc.Block() as blk:
                    @blk.sync
                    def _(s, off=off, w=w):
                        inst = s.dma_start(
                            out=xf32.rearrange("p (c n) -> p c n", c=NCH)[:, :, 0:w],
                            in_=dram_tiles(io["x_ext"], off, w))
                        p.sl.inc(inst, 16)
                    emit_ln_tile(p, blk, ln,
                                 lambda c, w=w: xf32[:, c * 512:c * 512 + w],
                                 lambda c, off=off, w=w: xn_ap(c, off, w), w)
                p.bar()

        if "dbg_xn" in io:
            with nc.Block() as blk:
                @blk.sync
                def _(s):
                    inst = s.dma_start(out=dram_tiles(io["dbg_xn"], 0, WE),
                                       in_=xn016.rearrange("p (c n) -> p c n", c=NCH))
                    p.st.inc(inst, 16)
            p.bar()

        with contextlib.ExitStack() as s1:
            qrow = [s1.enter_context(nc.sbuf_tensor(f"a_qr{i}", [128, WE], F16))
                    for i in range(2)]
            st_base = p.st.v

            def q_evac(m, mi, ps, pe_tok, ev_toks):
                @blk.scalar
                def _(s, m=m, mi=mi, ps=ps, pe_tok=pe_tok):
                    _w(s, p.pe, pe_tok)
                    if mi >= 2:
                        _w(s, p.st, st_base + 16 * (mi - 1))
                    for ti, (off, w) in enumerate(TILES):
                        inst = s.activation(qrow[mi % 2][:, off:off + w],
                                            ps[ti][0:128, 0:w], AF.Relu)
                    p.ac.inc(inst)
                ev_toks.append((p.ac, p.ac.v))

                @blk.sync
                def _(s, m=m, mi=mi):
                    _w(s, p.ac)
                    inst = s.dma_start(out=io["q16"][m * 128:(m + 1) * 128, :],
                                       in_=qrow[mi % 2][:])
                    p.st.inc(inst, 16)

            with nc.Block() as blk:
                emit_proj(p, blk, s1, tag="q", Ms=list(range(16)), nK=16,
                          wap=lambda m: wslice(io["wqkvT"], m),
                          rhs=lambda k, off, w: xn_ap(k, off, w),
                          tiles=TILES, evac=q_evac)
            p.bar()

        with contextlib.ExitStack() as s2:
            k16 = s2.enter_context(nc.sbuf_tensor("a_k16", [128, 8 * D], F16))
            v16 = s2.enter_context(nc.sbuf_tensor("a_v16", [128, 8 * D], F16))
            wkv = [s2.enter_context(nc.sbuf_tensor(f"a_wkv{i}", [128, 16 * 512], F16))
                   for i in range(2)]
            vk_sb = s2.enter_context(nc.sbuf_tensor("a_vksb", [128, 2048], F32))
            ksum_sb = s2.enter_context(nc.sbuf_tensor("a_kssb", [1, 2048], F32))
            ones16 = s2.enter_context(nc.sbuf_tensor("a_o16", [128, 1], F16))
            pg = [s2.enter_context(nc.psum_tensor(f"a_pg{i}", [128, 512], F32))
                  for i in range(8)]
            with nc.Block() as blk:
                @blk.vector
                def _(v):
                    v.memset(vk_sb[:], 0.0)
                    v.memset(ksum_sb[:], 0.0)
                    inst = v.memset(ones16[:], 1.0)
                    p.dv.inc(inst)
            p.bar()

            for gh in range(2):
                gl_base = p.gl.v
                pe_base = p.pe.v
                ev = []
                with nc.Block() as blk:
                    @blk.gpsimd
                    def _(g):
                        for fi in range(8):
                            if fi >= 2:
                                g.wait_ge(p.pe.h, pe_base + fi - 1)
                            col = 2048 + fi * 512 if fi < 4 else 4096 + (fi - 4) * 512
                            inst = g.dma_start(out=wkv[fi % 2][:],
                                               in_=wslice(io["wqkvT"], col // 512, mw=512))
                            p.gl.inc(inst, 16)

                    for fi in range(8):
                        @blk.tensor
                        def _(t, fi=fi):
                            _w(t, p.gl, gl_base + 16 * (fi + 1))
                            if fi >= 1:
                                s_, v_ = ev[fi - 1]
                                t.wait_ge(s_.h, v_)
                            inst = None
                            for k in range(16):
                                for g in range(8):
                                    gcol = HALO + (gh * 8 + g) * 128
                                    inst = t.matmul(
                                        pg[g][0:128, 0:512],
                                        xn016[:, k * WE + gcol:k * WE + gcol + 128],
                                        wkv[fi % 2][:, k * 512:(k + 1) * 512],
                                        start=(k == 0), stop=(k == 15),
                                        skip_group_check=True)
                            p.pe.inc(inst)
                        pe_tok = p.pe.v

                        if fi < 4:
                            @blk.scalar
                            def _(s, fi=fi, pe_tok=pe_tok):
                                _w(s, p.pe, pe_tok)
                                for g in range(8):
                                    inst = s.activation(
                                        k16[:, g * D + fi * 512: g * D + (fi + 1) * 512],
                                        pg[g][0:128, 0:512], AF.Relu)
                                p.ac.inc(inst)
                            ev.append((p.ac, p.ac.v))
                        else:
                            @blk.vector
                            def _(v, fi=fi, pe_tok=pe_tok):
                                _w(v, p.pe, pe_tok)
                                fo = (fi - 4) * 512
                                for g in range(8):
                                    inst = v.tensor_copy(
                                        v16[:, g * D + fo: g * D + fo + 512],
                                        pg[g][0:128, 0:512])
                                p.dv.inc(inst)
                            ev.append((p.dv, p.dv.v))
                p.bar()

                with nc.Block() as blk:
                    @blk.tensor
                    def _(t):
                        inst = None
                        for g in range(8):
                            for cl in range(16):
                                b, sl = cl // 4, cl % 4
                                inst = t.matmul(
                                    pg[b][0:128, sl * 128:(sl + 1) * 128],
                                    k16[:, g * D + cl * 128: g * D + (cl + 1) * 128],
                                    v16[:, g * D + cl * 128: g * D + (cl + 1) * 128],
                                    start=(g == 0 and sl == 0), stop=(g == 7 and sl == 3),
                                    skip_group_check=True)
                            for fc in range(4):
                                inst = t.matmul(
                                    pg[4 + fc][0:1, 0:512],
                                    ones16[0:128, 0:1],
                                    k16[:, g * D + fc * 512: g * D + (fc + 1) * 512],
                                    start=(g == 0), stop=(g == 7), skip_group_check=True)
                        p.pe.inc(inst)
                    pe_vk = p.pe.v

                    @blk.vector
                    def _(v):
                        _w(v, p.pe, pe_vk)
                        for b in range(4):
                            v.tensor_add(vk_sb[:, b * 512:(b + 1) * 512],
                                         vk_sb[:, b * 512:(b + 1) * 512],
                                         pg[b][0:128, 0:512])
                        for fc in range(4):
                            inst = v.tensor_add(ksum_sb[0:1, fc * 512:(fc + 1) * 512],
                                                ksum_sb[0:1, fc * 512:(fc + 1) * 512],
                                                pg[4 + fc][0:1, 0:512])
                        p.dv.inc(inst)
                p.bar()

            with nc.Block() as blk:
                @blk.sync
                def _(s):
                    vkv = io["vkbuf"]
                    for hm4 in range(4):
                        src = vk_sb[hm4 * 32:(hm4 + 1) * 32, :]
                        src = src.rearrange("p (cl x) -> p cl x", x=128)[:, :, hm4 * 32:(hm4 + 1) * 32]
                        dst = vkv[0:VKN].rearrange("(cl j d e) -> j d cl e",
                                                   cl=16, j=4, d=32)[hm4]
                        inst = s.dma_start(out=dst, in_=src)
                        p.st.inc(inst, 16)
                    inst = s.dma_start(out=vkv[VKN:VKN + 2048].rearrange("(a n) -> a n", a=1),
                                       in_=ksum_sb[:])
                    p.st.inc(inst, 16)
                    _w(s, p.st)
            p.bar()

        with contextlib.ExitStack() as s3:
            hsl = [s3.enter_context(nc.sbuf_tensor(f"a_h{j}", [128, 2052], F16))
                   for j in range(4)]
            hu = s3.enter_context(nc.sbuf_tensor("a_hu", [128, 2052], F16))
            asl = [s3.enter_context(nc.sbuf_tensor(f"a_a{j}", [128, 2052], F16))
                   for j in range(4)]
            tA = s3.enter_context(nc.sbuf_tensor("a_tA", [128, 2052], F16))
            tB = s3.enter_context(nc.sbuf_tensor("a_tB", [128, 2052], F16))

            Ms = []
            for gc in range(16):
                for j in range(4):
                    Ms.append((gc, j, 0))
                    Ms.append((gc, j, 1))

            def wap_ffn0(m):
                gc, j, u = m
                wd = io["wu0"] if u else io["wg0"]
                return wslice(wd, j * 16 + gc)

            def evac_ffn0(m, mi, ps, pe_tok, ev_toks):
                gc, j, u = m
                if u == 0:
                    @blk.scalar
                    def _(s, j=j, ps=ps, pe_tok=pe_tok):
                        _w(s, p.pe, pe_tok)
                        _w(s, p.dv)
                        _w(s, p.st)
                        for ti, (off, w) in enumerate(TILES):
                            inst = s.activation(hsl[j][:, off:off + w],
                                                ps[ti][0:128, 0:w], AF.Silu)
                        p.ac.inc(inst)
                    ev_toks.append((p.ac, p.ac.v))
                else:
                    @blk.vector
                    def _(v, j=j, ps=ps, pe_tok=pe_tok):
                        _w(v, p.pe, pe_tok)
                        _w(v, p.st)
                        for ti, (off, w) in enumerate(TILES):
                            inst = v.tensor_copy(hu[:, off:off + w], ps[ti][0:128, 0:w])
                        p.dv.inc(inst)
                    ev_toks.append((p.dv, p.dv.v))

                    @blk.vector
                    def _(v, gc=gc, j=j):
                        _w(v, p.ac)
                        inst = v.tensor_mul(hsl[j][:], hsl[j][:], hu[:])
                        p.dv.inc(inst)

                    if j == 3:
                        @blk.vector
                        def _(v, gc=gc):
                            for jj in range(4):
                                v.tensor_mul(asl[jj][:], hsl[jj][:], hsl[jj][:])
                            v.tensor_tensor(tA[:], asl[0][:], asl[1][:], ALU.max)
                            v.tensor_tensor(tB[:], asl[2][:], asl[3][:], ALU.max)
                            v.tensor_tensor(tA[:], tA[:], tB[:], ALU.min)
                            v.tensor_tensor(tB[:], asl[0][:], asl[1][:], ALU.min)
                            v.tensor_tensor(hu[:], asl[2][:], asl[3][:], ALU.min)
                            v.tensor_tensor(tB[:], tB[:], hu[:], ALU.max)
                            v.tensor_tensor(tA[:], tA[:], tB[:], ALU.max)
                            for jj in range(4):
                                v.tensor_tensor(asl[jj][:], asl[jj][:], tA[:], ALU.is_ge)
                            for jj in range(4):
                                inst = v.tensor_mul(hsl[jj][:], hsl[jj][:], asl[jj][:])
                            p.dv.inc(inst)

                        @blk.sync
                        def _(s, gc=gc):
                            _w(s, p.dv)
                            for jj in range(4):
                                row = jj * 2048 + gc * 128
                                inst = s.dma_start(
                                    out=io["hm0"][row:row + 128, :],
                                    in_=hsl[jj][:, 0:WE])
                                p.st.inc(inst, 16)

            with nc.Block() as blk:
                emit_proj(p, blk, s3, tag="f0a", Ms=Ms, nK=16,
                          wap=wap_ffn0, rhs=lambda k, off, w: xn_ap(k, off, w),
                          tiles=TILES, evac=evac_ffn0)
            p.bar()

    with contextlib.ExitStack() as s4:
        nc = p.nc
        hmsb = s4.enter_context(nc.sbuf_tensor("a_hm", [128, ICH * 1027], F16))
        g0r = [s4.enter_context(nc.sbuf_tensor(f"a_g0{i}", [128, 1027], F16))
               for i in range(2)]
        slabs = [(0, 1027, [(1024, 3), (0, 512), (512, 512)]),
                 (1027, 1024, [(0, 512), (512, 512)])]
        for sbase, sw, subs in slabs:
            with nc.Block() as blk:
                @blk.sync
                def _(s, sbase=sbase, sw=sw):
                    _w(s, p.st)
                    inst = s.dma_start(
                        out=hmsb.rearrange("p (c n) -> p c n", c=ICH)[:, :, 0:sw],
                        in_=dram_tiles64(io["hm0"], sbase, sw))
                    p.sl.inc(inst, 16)
            p.bar()
            st_base = p.st.v

            def evac_d0(m, mi, ps, pe_tok, ev_toks, sbase=sbase, subs=subs):
                @blk.scalar
                def _(s, m=m, mi=mi, ps=ps, pe_tok=pe_tok):
                    _w(s, p.pe, pe_tok)
                    if mi >= 2:
                        _w(s, p.st, st_base + 16 * (mi - 1))
                    for ti, (off, w) in enumerate(subs):
                        inst = s.activation(g0r[mi % 2][:, off:off + w],
                                            ps[ti][0:128, 0:w], AF.Silu)
                    p.ac.inc(inst)
                ev_toks.append((p.ac, p.ac.v))

                @blk.sync
                def _(s, m=m, mi=mi, sbase=sbase):
                    _w(s, p.ac)
                    sw_ = max(o + ww for o, ww in subs)
                    inst = s.dma_start(
                        out=io["g016"][m * 128:(m + 1) * 128, sbase:sbase + sw_],
                        in_=g0r[mi % 2][:, 0:sw_])
                    p.st.inc(inst, 16)

            with contextlib.ExitStack() as ss, nc.Block() as blk:
                emit_proj(p, blk, ss, tag=f"f0b{sbase}", Ms=list(range(16)), nK=ICH,
                          wap=lambda m: wslice(io["wd0"], m),
                          rhs=lambda k, off, w: hmsb[:, k * 1027 + off:k * 1027 + off + w],
                          tiles=subs, evac=evac_d0, pe_pre=[(p.sl, p.sl.v)])
            p.bar()



def emit_vk_reduce(p, io, split):
    nc = p.nc
    if split:
        with contextlib.ExitStack() as stk:
            pt = [stk.enter_context(nc.sbuf_tensor(f"b_vkp{i}", [128, 528], F32))
                  for i in range(4)]
            with nc.Block() as blk:
                @blk.sync
                def _(s):
                    for i in range(4):
                        inst = s.dma_start(
                            out=pt[i][:],
                            in_=io["vkparts"][i, :].rearrange("(p n) -> p n", p=128))
                        p.sl.inc(inst, 16)
                @blk.vector
                def _(v):
                    _w(v, p.sl)
                    v.tensor_add(pt[0][:], pt[0][:], pt[1][:])
                    v.tensor_add(pt[2][:], pt[2][:], pt[3][:])
                    inst = v.tensor_add(pt[0][:], pt[0][:], pt[2][:])
                    p.dv.inc(inst)
                @blk.gpsimd
                def _(g):
                    _w(g, p.dv)
                    inst = g.dma_start(
                        out=io["vkbuf"].rearrange("(p n) -> p n", p=128),
                        in_=pt[0][:])
                    p.gl.inc(inst, 16)
                    _w(g, p.gl)
            p.bar()
    else:
        with nc.Block() as blk:
            @blk.gpsimd
            def _(g):
                _w(g, p.st)
                flat = io["vkbuf"][None, :]
                inst = g.collective_compute("AllReduce", ALU.add,
                                            replica_groups=GROUPS,
                                            ins=[flat], outs=[flat])
                p.cc.inc(inst, 1)
                _w(g, p.cc)
        p.bar()


def phase_b_full(p, io, split):
    nc = p.nc
    emit_vk_reduce(p, io, split)

    with contextlib.ExitStack() as stk:
        zt = stk.enter_context(nc.sbuf_tensor("b_zero", [128, 2048], F16))
        with nc.Block() as blk:
            @blk.vector
            def _(v):
                inst = v.memset(zt[:], 0.0)
                p.dv.inc(inst)
            @blk.gpsimd
            def _(g):
                _w(g, p.dv)
                inst = g.dma_start(
                    out=io["BD16"].rearrange("(c p e) -> p c e", p=128, c=16),
                    in_=zt.rearrange("p (c e) -> p c e", c=16))
                p.gl.inc(inst, 16)
                _w(g, p.gl)
                for j in range(4):
                    src = io["vkbuf"][0:VKN].rearrange("(c j d e) -> c j d e",
                                                       c=16, j=4, d=32)[:, j, :, 0:32]
                    dst = io["BD16"].rearrange("(c p e) -> c p e", c=16, p=128)
                    dst = dst[:, j * 32:(j + 1) * 32, j * 32:(j + 1) * 32]
                    inst = g.dma_start(out=dst, in_=src)
                    p.gl.inc(inst, 16)
                _w(g, p.gl)
        p.bar()

    with contextlib.ExitStack() as stk:
        bd16 = stk.enter_context(nc.sbuf_tensor("b_bd", [128, 16 * 128], F16))
        denl = stk.enter_context(nc.sbuf_tensor("b_denl", [128, 16 * 64], F16))
        ind16 = stk.enter_context(nc.sbuf_tensor("b_ind", [128, 16 * 64], F16))
        bbc32 = stk.enter_context(nc.sbuf_tensor("b_bbc", [64, 2048], F32))
        ksum = stk.enter_context(nc.sbuf_tensor("b_ks", [128, 16], F32))
        att16 = stk.enter_context(nc.sbuf_tensor("b_att", [128, NCH * WE], F16))

        with contextlib.ExitStack() as s1:
            q16sb = s1.enter_context(nc.sbuf_tensor("b_q", [128, NCH * 512], F16))
            dr = s1.enter_context(nc.sbuf_tensor("b_dr", [64, 512], F32))
            bc32 = s1.enter_context(nc.sbuf_tensor("b_bc", [128, 512], F32))
            ps_den = s1.enter_context(nc.psum_tensor("b_pd", [128, 512], F32))
            ps_n = [s1.enter_context(nc.psum_tensor(f"b_pn{i}", [128, 512], F32))
                    for i in range(2)]
            ps_b = [s1.enter_context(nc.psum_tensor(f"b_pb{i}", [128, 512], F32))
                    for i in range(2)]

            with nc.Block() as blk:
                @blk.gpsimd
                def _(g):
                    inst = g.dma_start(out=bd16[:],
                                       in_=io["BD16"].rearrange("(c p e) -> p c e",
                                                                p=128, c=16))
                    p.gl.inc(inst, 16)
                    inst = g.dma_start(out=ind16[:],
                                       in_=dram_tiles(io["indden"], 0, 64))
                    p.gl.inc(inst, 16)
                @blk.sync
                def _(s):
                    _w(s, p.gl)
                    inst = s.dma_start(out=bbc32[:], in_=io["bbc"][:, :])
                    p.sl.inc(inst, 16)
                    for c in range(NCH):
                        inst = s.dma_start(
                            out=ksum[:, c:c + 1],
                            in_=io["vkbuf"][VKN + c * 128: VKN + (c + 1) * 128][:, None])
                        p.sl.inc(inst, 16)
                @blk.vector
                def _(v):
                    _w(v, p.sl)
                    _w(v, p.gl)
                    for c in range(NCH):
                        inst = v.scalar_tensor_tensor(
                            denl[:, c * 64:(c + 1) * 64], ind16[:, c * 64:(c + 1) * 64],
                            ksum[:, c:c + 1], ind16[:, c * 64:(c + 1) * 64],
                            ALU.mult, ALU.mult)
                    p.dv.inc(inst)
            p.bar()

            for (off, w) in TILES:
                with nc.Block() as blk:
                    @blk.sync
                    def _(s, off=off, w=w):
                        _w(s, p.st)
                        inst = s.dma_start(
                            out=q16sb.rearrange("p (c n) -> p c n", c=NCH)[:, :, 0:w],
                            in_=dram_tiles(io["q16"], off, w))
                        p.sl.inc(inst, 16)
                    sl_q = p.sl.v

                    @blk.tensor
                    def _(t, off=off, w=w):
                        _w(t, p.sl, sl_q)
                        for c in range(NCH):
                            inst = t.matmul(ps_den[0:64, 0:w],
                                            denl[:, c * 64:(c + 1) * 64],
                                            q16sb[:, c * 512:c * 512 + w],
                                            start=(c == 0), stop=(c == NCH - 1),
                                            skip_group_check=True)
                        p.pe.inc(inst)
                    pe_den = p.pe.v

                    @blk.vector
                    def _(v, off=off, w=w):
                        _w(v, p.pe, pe_den)
                        inst = v.tensor_scalar_add(dr[0:64, 0:w], ps_den[0:64, 0:w], 1e-15)
                        p.dv.inc(inst)
                        v.wait_ge(p.dv.h, p.dv.v)
                        inst = v.reciprocal(dr[0:64, 0:w], dr[0:64, 0:w])
                        p.dv.inc(inst)
                    dv_dr = p.dv.v

                    for c in range(NCH):
                        @blk.tensor
                        def _(t, c=c, off=off, w=w):
                            if c == 0:
                                _w(t, p.dv, dv_dr)
                            else:
                                _w(t, p.dv)
                            t.matmul(ps_b[c % 2][0:128, 0:w],
                                     bbc32[:, c * 128:(c + 1) * 128],
                                     dr[0:64, 0:w], start=True, stop=True)
                            inst = t.matmul(ps_n[c % 2][0:128, 0:w],
                                            bd16[:, c * 128:(c + 1) * 128],
                                            q16sb[:, c * 512:c * 512 + w],
                                            start=True, stop=True)
                            p.pe.inc(inst)
                        pe_c = p.pe.v

                        @blk.scalar
                        def _(s, c=c, off=off, w=w):
                            _w(s, p.pe, pe_c)
                            inst = s.activation(bc32[:, 0:w], ps_b[c % 2][0:128, 0:w],
                                                AF.Copy)
                            p.ac.inc(inst)
                        ac_c = p.ac.v

                        @blk.vector
                        def _(v, c=c, off=off, w=w):
                            _w(v, p.ac, ac_c)
                            inst = v.tensor_mul(att16[:, c * WE + off:c * WE + off + w],
                                                ps_n[c % 2][0:128, 0:w], bc32[:, 0:w])
                            p.dv.inc(inst)
                p.bar()

        with contextlib.ExitStack() as s2:
            s0r = s2.enter_context(nc.sbuf_tensor("b_s0", [128, WE], F16))
            g0sb = s2.enter_context(nc.sbuf_tensor("b_g0", [128, WE], F16))
            xrow = s2.enter_context(nc.sbuf_tensor("b_xr", [128, WE], F32))
            t16 = s2.enter_context(nc.sbuf_tensor("b_t16", [128, WE], F16))
            x1r = s2.enter_context(nc.sbuf_tensor("b_x1", [128, WE], F32))
            bpr = s2.enter_context(nc.sbuf_tensor("b_bp", [128, 16], F32))
            with nc.Block() as blk:
                @blk.sync
                def _(s):
                    inst = s.dma_start(out=bpr[:], in_=io["bproj_r"][:, :])
                    p.sl.inc(inst, 16)
            p.bar()

            def evac_proj(m, mi, ps, pe_tok, ev_toks):
                @blk.sync
                def _(s, m=m):
                    _w(s, p.dv)
                    _w(s, p.st)
                    inst = s.dma_start(out=g0sb[:], in_=io["g016"][m * 128:(m + 1) * 128, :])
                    p.sl.inc(inst, 16)
                    inst = s.dma_start(out=xrow[:], in_=io["x_ext"][m * 128:(m + 1) * 128, :])
                    p.sl.inc(inst, 16)
                sl_f = p.sl.v

                @blk.scalar
                def _(s, m=m, ps=ps, pe_tok=pe_tok):
                    _w(s, p.pe, pe_tok)
                    for ti, (off, w) in enumerate(TILES):
                        inst = s.activation(s0r[:, off:off + w], ps[ti][0:128, 0:w],
                                            AF.Identity, bias=bpr[:, m:m + 1])
                    p.ac.inc(inst)
                ev_toks.append((p.ac, p.ac.v))
                ac_f = p.ac.v

                @blk.vector
                def _(v, m=m):
                    _w(v, p.ac, ac_f)
                    _w(v, p.sl, sl_f)
                    v.tensor_mul(t16[:], s0r[:], g0sb[:])
                    inst = v.tensor_add(x1r[:], xrow[:], t16[:])
                    p.dv.inc(inst)
                dv_f = p.dv.v

                @blk.sync
                def _(s, m=m):
                    _w(s, p.dv, dv_f)
                    inst = s.dma_start(out=io["x1s"][m * 128:(m + 1) * 128, :], in_=x1r[:])
                    p.st.inc(inst, 16)

            with nc.Block() as blk:
                emit_proj(p, blk, s2, tag="bproj", Ms=list(range(16)), nK=16,
                          wap=lambda m: wslice(io["wprojT"], m),
                          rhs=lambda k, off, w: att16[:, k * WE + off:k * WE + off + w],
                          tiles=TILES, evac=evac_proj)
            p.bar()

    with contextlib.ExitStack() as stk:
        xf32 = stk.enter_context(nc.sbuf_tensor("b_lx", [128, NCH * 512], F32))
        xnr = stk.enter_context(nc.sbuf_tensor("b_lo", [128, NCH * 512], F16))
        ln = alloc_ln(p, stk, "b")
        with nc.Block() as blk:
            emit_ln_init(p, blk, ln)
        for (off, w) in TILES:
            with nc.Block() as blk:
                @blk.sync
                def _(s, off=off, w=w):
                    _w(s, p.st)
                    inst = s.dma_start(
                        out=xf32.rearrange("p (c n) -> p c n", c=NCH)[:, :, 0:w],
                        in_=dram_tiles(io["x1s"], off, w))
                    p.sl.inc(inst, 16)
                emit_ln_tile(p, blk, ln,
                             lambda c, w=w: xf32[:, c * 512:c * 512 + w],
                             lambda c, w=w: xnr[:, c * 512:c * 512 + w], w)
                @blk.sync
                def _(s, off=off, w=w):
                    _w(s, p.dv)
                    inst = s.dma_start(
                        out=dram_tiles(io["xn116s"], off, w),
                        in_=xnr.rearrange("p (c n) -> p c n", c=NCH)[:, :, 0:w])
                    p.st.inc(inst, 16)
            p.bar()



def phase_c(p, io, split, parts="3645"):
    nc = p.nc
    with contextlib.ExitStack() as stk:
        sstage = stk.enter_context(nc.sbuf_tensor("c_ss", [16, T], F32))
        convw = stk.enter_context(nc.sbuf_tensor("c_cw", [128, 64], F32))
        convb = stk.enter_context(nc.sbuf_tensor("c_cb", [128, 16], F32))
        nsc = stk.enter_context(nc.sbuf_tensor("c_ns", [128, 16], F32))
        dtb = stk.enter_context(nc.sbuf_tensor("c_db", [128, 16], F32))
        ident = stk.enter_context(nc.sbuf_tensor("c_id", [128, 128], F32))

        cstk = stk.enter_context(contextlib.ExitStack())
        xc16 = cstk.enter_context(nc.sbuf_tensor("c_xc", [128, NCH * T], F16))
        cos16 = cstk.enter_context(nc.sbuf_tensor("c_cos", [64, T], F16))
        sin16 = cstk.enter_context(nc.sbuf_tensor("c_sin", [64, T], F16))
        with nc.Block() as blk:
            @blk.gpsimd
            def _(g):
                inst = g.dma_start(out=cos16[:], in_=io["cos_t"][:, :])
                p.gl.inc(inst, 16)
                inst = g.dma_start(out=sin16[:], in_=io["sin_t"][:, :])
                p.gl.inc(inst, 16)

            @blk.sync
            def _(s):
                for nm, t in (("convw_r", convw), ("convb_r", convb),
                              ("nscale_r", nsc), ("dtb_r", dtb), ("ident", ident)):
                    inst = s.dma_start(out=t[:], in_=io[nm][:, :])
                    p.sl.inc(inst, 16)
        io["_gl_cos"] = p.gl.v
        p.bar()

        with contextlib.ExitStack() as s1:
            xn116 = s1.enter_context(nc.sbuf_tensor("c_xn1", [128, NCH * WE], F16))
            rmsb = s1.enter_context(nc.sbuf_tensor("c_rms", [128, WE], F32))
            sqt = s1.enter_context(nc.sbuf_tensor("c_sq", [128, NCH * 512], F16))
            o16 = s1.enter_context(nc.sbuf_tensor("c_o16", [128, 1], F16))
            o1f = s1.enter_context(nc.sbuf_tensor("c_o1f", [1, 128], F32))
            srow = s1.enter_context(nc.sbuf_tensor("c_sr", [1, 1024], F32))
            ps_q = s1.enter_context(nc.psum_tensor("c_pq", [128, 512], F32))
            ps_b = s1.enter_context(nc.psum_tensor("c_pb", [128, 512], F32))

            with nc.Block() as blk:
                @blk.sync
                def _(s):
                    _w(s, p.st)
                    inst = s.dma_start(
                        out=xn116.rearrange("p (c n) -> p c n", c=NCH),
                        in_=dram_tiles(io["xn116s"], 0, WE))
                    p.sl.inc(inst, 16)
                @blk.vector
                def _(v):
                    v.memset(o16[:], 1.0)
                    inst = v.memset(o1f[:], 1.0)
                    p.dv.inc(inst)
            p.bar()

            for (off, w) in TILES:
                with nc.Block() as blk:
                    @blk.scalar
                    def _(s, off=off, w=w):
                        _w(s, p.dv)
                        _w(s, p.sl)
                        for c in range(NCH):
                            inst = s.activation(sqt[:, c * 512:c * 512 + w],
                                                xn116[:, c * WE + off:c * WE + off + w],
                                                AF.Square)
                        p.ac.inc(inst)
                    ac_sq = p.ac.v

                    @blk.tensor
                    def _(t, off=off, w=w):
                        _w(t, p.ac, ac_sq)
                        _w(t, p.dv)
                        for c in range(NCH):
                            inst = t.matmul(ps_q[0:1, 0:w], o16[0:128, 0:1],
                                            sqt[:, c * 512:c * 512 + w],
                                            start=(c == 0), stop=(c == NCH - 1),
                                            skip_group_check=True)
                        p.pe.inc(inst)
                    pe_q = p.pe.v

                    @blk.scalar
                    def _(s, off=off, w=w):
                        _w(s, p.pe, pe_q)
                        inst = s.activation(srow[0:1, 0:w], ps_q[0:1, 0:w], AF.Copy,
                                            scale=1.0 / 2048.0)
                        p.ac.inc(inst)
                    ac_m = p.ac.v

                    @blk.vector
                    def _(v, off=off, w=w):
                        _w(v, p.ac, ac_m)
                        inst = v.tensor_scalar_add(srow[0:1, 0:w], srow[0:1, 0:w], 1e-6)
                        p.dv.inc(inst)
                        v.wait_ge(p.dv.h, p.dv.v)
                        inst = v.reciprocal(srow[0:1, 512:512 + w], srow[0:1, 0:w])
                        p.dv.inc(inst)
                    dv_r = p.dv.v

                    @blk.scalar
                    def _(s, off=off, w=w):
                        _w(s, p.dv, dv_r)
                        inst = s.activation(srow[0:1, 0:w], srow[0:1, 512:512 + w], AF.Sqrt)
                        p.ac.inc(inst)
                    ac_r = p.ac.v

                    @blk.tensor
                    def _(t, off=off, w=w):
                        _w(t, p.ac, ac_r)
                        inst = t.matmul(ps_b[0:128, 0:w], o1f[0:1, 0:128],
                                        srow[0:1, 0:w], start=True, stop=True)
                        p.pe.inc(inst)
                    pe_b = p.pe.v

                    @blk.vector
                    def _(v, off=off, w=w):
                        _w(v, p.pe, pe_b)
                        inst = v.tensor_copy(rmsb[:, off:off + w], ps_b[0:128, 0:w])
                        p.dv.inc(inst)
                p.bar()

            with contextlib.ExitStack() as s2:
                tmp32 = s2.enter_context(nc.sbuf_tensor("c_t32", [128, WE], F32))
                xnn16 = s2.enter_context(nc.sbuf_tensor("c_xnn", [128, WE], F16))
                ping = s2.enter_context(nc.sbuf_tensor("c_pg", [128, T], F32))
                pong = s2.enter_context(nc.sbuf_tensor("c_po", [128, T], F32))
                for c in range(NCH):
                    with nc.Block() as blk:
                        @blk.vector
                        def _(v, c=c):
                            _w(v, p.ac)
                            v.scalar_tensor_tensor(tmp32[:], xn116[:, c * WE:(c + 1) * WE],
                                                   nsc[:, c:c + 1], rmsb[:],
                                                   ALU.mult, ALU.mult)
                            v.tensor_scalar(xnn16[:], tmp32[:], 60000.0, -60000.0,
                                            ALU.min, ALU.max)
                            v.tensor_scalar_mul(ping[:], xnn16[:, 0:T], convw[:, c * 4:c * 4 + 1])
                            v.scalar_tensor_tensor(pong[:], xnn16[:, 1:1 + T],
                                                   convw[:, c * 4 + 1:c * 4 + 2], ping[:],
                                                   ALU.mult, ALU.add)
                            v.scalar_tensor_tensor(ping[:], xnn16[:, 2:2 + T],
                                                   convw[:, c * 4 + 2:c * 4 + 3], pong[:],
                                                   ALU.mult, ALU.add)
                            inst = v.scalar_tensor_tensor(pong[:], xnn16[:, 3:3 + T],
                                                          convw[:, c * 4 + 3:c * 4 + 4], ping[:],
                                                          ALU.mult, ALU.add)
                            p.dv.inc(inst)
                        dv_c = p.dv.v

                        @blk.scalar
                        def _(s, c=c):
                            _w(s, p.dv, dv_c)
                            inst = s.activation(xc16[:, c * T:(c + 1) * T], pong[:],
                                                AF.Silu, bias=convb[:, c:c + 1])
                            p.ac.inc(inst)
            p.bar()

        if "dbg_xc" in io:
            with nc.Block() as blk:
                @blk.sync
                def _(s):
                    _w(s, p.ac)
                    inst = s.dma_start(out=dram_tiles(io["dbg_xc"], 0, T),
                                       in_=xc16.rearrange("p (c n) -> p c n", c=NCH))
                    p.st.inc(inst, 16)
            p.bar()

        if "3" not in parts:
            cstk.close()
            return

        with contextlib.ExitStack() as s3:
            xb32 = s3.enter_context(nc.sbuf_tensor("c_xb", [128, T], F32))
            dt32 = s3.enter_context(nc.sbuf_tensor("c_dt", [128, T], F32))
            vin32 = s3.enter_context(nc.sbuf_tensor("c_vi", [128, T], F32))
            L32 = s3.enter_context(nc.sbuf_tensor("c_L", [128, T], F32))
            Lc32 = s3.enter_context(nc.sbuf_tensor("c_Lc", [128, T], F32))
            Lsh32 = s3.enter_context(nc.sbuf_tensor("c_Ls", [128, T], F32))
            dec32 = s3.enter_context(nc.sbuf_tensor("c_de", [128, T], F32))
            co32 = s3.enter_context(nc.sbuf_tensor("c_co", [128, T], F32))
            v16 = s3.enter_context(nc.sbuf_tensor("c_v16", [128, T], F16))
            vr16 = s3.enter_context(nc.sbuf_tensor("c_vr", [128, T], F16))
            tt16 = s3.enter_context(nc.sbuf_tensor("c_tt", [128, T], F16))
            vt2 = s3.enter_context(nc.sbuf_tensor("c_vt2", [64, T], F16))
            tb2 = s3.enter_context(nc.sbuf_tensor("c_tb2", [64, T], F16))
            eLr = s3.enter_context(nc.sbuf_tensor("c_eL", [128, T], F16))
            co16st = s3.enter_context(nc.sbuf_tensor("c_co16", [128, T], F16))
            gtr = s3.enter_context(nc.sbuf_tensor("c_gt", [128, T], F16))
            bdbo = s3.enter_context(nc.sbuf_tensor("c_bb", [128, NCH * 16], F32))
            ones256 = s3.enter_context(nc.sbuf_tensor("c_o256", [128, 256], F32))
            ps_tp = s3.enter_context(nc.psum_tensor("c_ptp", [128, 128], F32))

            with nc.Block() as blk:
                @blk.vector
                def _(v):
                    inst = v.memset(ones256[:], 1.0)
                    p.dv.inc(inst)
            p.bar()

            CT = [(512 * i, 512) for i in range(4)]
            Ms = []
            for c in range(NCH):
                Ms.append((c, "dt"))
                Ms.append((c, "v"))
                Ms.append((c, "gate"))

            def wap_c3(m):
                c, kind = m
                base = {"dt": 0, "v": 16, "gate": 32}[kind]
                return wslice(io["winT"], base + c)

            def evac_c3(m, mi, ps, pe_tok, ev_toks):
                c, kind = m
                if kind == "dt":
                    @blk.scalar
                    def _(s, c=c, ps=ps, pe_tok=pe_tok):
                        _w(s, p.pe, pe_tok)
                        _w(s, p.dv)
                        for ti, (off, w) in enumerate(CT):
                            inst = s.activation(xb32[:, off:off + w], ps[ti][0:128, 0:w],
                                                AF.Identity, bias=dtb[:, c:c + 1])
                        p.ac.inc(inst)
                    ev_toks.append((p.ac, p.ac.v))
                    ac_xb = p.ac.v
                    if C3LVL < 1:
                        return

                    @blk.vector
                    def _(v, c=c):
                        _w(v, p.ac, ac_xb)
                        v.tensor_scalar_mul(Lsh32[:], xb32[:], -1.0)
                        inst = v.tensor_tensor(Lsh32[:], xb32[:], Lsh32[:], ALU.min)
                        p.dv.inc(inst)
                    dv_a = p.dv.v

                    @blk.scalar
                    def _(s, c=c):
                        _w(s, p.dv, dv_a)
                        s.activation(dec32[:], Lsh32[:], AF.Exp)
                        inst = s.activation(Lsh32[:], dec32[:], AF.Ln, bias=1.0)
                        p.ac.inc(inst)
                    ac_ln = p.ac.v

                    @blk.vector
                    def _(v, c=c):
                        _w(v, p.ac, ac_ln)
                        v.tensor_scalar_max(L32[:], xb32[:], 0.0)
                        v.tensor_add(dt32[:], L32[:], Lsh32[:])
                        inst = v.tensor_scalar(dt32[:], dt32[:], 0.001, 2.0,
                                               ALU.max, ALU.min)
                        p.dv.inc(inst)

                elif kind == "v":
                    @blk.vector
                    def _(v, c=c, ps=ps, pe_tok=pe_tok):
                        _w(v, p.pe, pe_tok)
                        for ti, (off, w) in enumerate(CT):
                            inst = v.tensor_copy(v16[:, off:off + w], ps[ti][0:128, 0:w])
                        p.dv.inc(inst)
                    ev_toks.append((p.dv, p.dv.v))
                    if C3LVL < 2:
                        return

                    @blk.vector
                    def _(v, c=c):
                        _w(v, p.gl, io["_gl_cos"])
                        v.tensor_copy(vt2[:], v16[64:128, :])
                        v.tensor_mul(tt16[0:64, :], v16[0:64, :], cos16[:])
                        v.tensor_mul(tb2[:], vt2[:], sin16[:])
                        v.tensor_sub(vr16[0:64, :], tt16[0:64, :], tb2[:])
                        v.tensor_mul(tt16[0:64, :], v16[0:64, :], sin16[:])
                        v.tensor_mul(tb2[:], vt2[:], cos16[:])
                        v.tensor_add(tt16[0:64, :], tt16[0:64, :], tb2[:])
                        v.tensor_copy(vr16[64:128, :], tt16[0:64, :])
                        inst = v.tensor_mul(vin32[:], vr16[:], dt32[:])
                        if C3LVL >= 3:
                            for sg in range(NSEG):
                                v.tensor_tensor_scan(L32[:, sg * 256:(sg + 1) * 256],
                                                     ones256[:], dt32[:, sg * 256:(sg + 1) * 256],
                                                     0.0, ALU.mult, ALU.subtract)
                            v.tensor_scalar(Lc32[:], L32[:], 0.0, -20.0, ALU.min, ALU.max)
                            v.tensor_copy(Lsh32[:, 1:T], Lc32[:, 0:T - 1])
                            v.memset(Lsh32[:, 0:T:256], 0.0)
                            inst = v.tensor_sub(L32[:], Lc32[:], Lsh32[:])
                        p.dv.inc(inst)
                    dv_sc = p.dv.v
                    if C3LVL < 4:
                        return

                    @blk.scalar
                    def _(s, c=c):
                        _w(s, p.dv, dv_sc)
                        _w(s, p.st)
                        s.activation(dec32[:], L32[:], AF.Exp)
                        inst = s.activation(eLr[:], Lc32[:], AF.Exp)
                        p.ac.inc(inst)
                    ac_ex = p.ac.v

                    @blk.vector
                    def _(v, c=c):
                        _w(v, p.ac, ac_ex)
                        for sg in range(NSEG):
                            v.tensor_tensor_scan(co32[:, sg * 256:(sg + 1) * 256],
                                                 dec32[:, sg * 256:(sg + 1) * 256],
                                                 vin32[:, sg * 256:(sg + 1) * 256],
                                                 0.0, ALU.mult, ALU.add)
                        v.tensor_copy(bdbo[:, c * 16:c * 16 + 8], Lc32[:, 255:T:256])
                        v.tensor_copy(bdbo[:, c * 16 + 8:c * 16 + 16],
                                      co32[:, 255:T:256])
                        _w(v, p.st)
                        inst = v.tensor_copy(co16st[:], co32[:])
                        p.dv.inc(inst)
                    dv_co = p.dv.v
                    if C3LVL < 5:
                        return

                    @blk.sync
                    def _(s, c=c, dv_co=dv_co):
                        _w(s, p.dv, dv_co)
                        inst = s.dma_start(out=io["co16"][c * 128:(c + 1) * 128, :],
                                           in_=co16st[:])
                        p.st.inc(inst, 16)
                        if "dbg_dt" in io:
                            inst = s.dma_start(out=io["dbg_dt"][c * 128:(c + 1) * 128, :],
                                               in_=dt32[:])
                            p.st.inc(inst, 16)
                            inst = s.dma_start(out=io["dbg_vin"][c * 128:(c + 1) * 128, :],
                                               in_=vin32[:])
                            p.st.inc(inst, 16)

                    if C3LVL >= 5:
                        @blk.sync
                        def _(s, c=c):
                            _w(s, p.ac)
                            inst = s.dma_start(out=io["eL16"][c * 128:(c + 1) * 128, :],
                                               in_=eLr[:])
                            p.st.inc(inst, 16)

                else:
                    @blk.scalar
                    def _(s, c=c, ps=ps, pe_tok=pe_tok):
                        _w(s, p.pe, pe_tok)
                        _w(s, p.st)
                        for ti, (off, w) in enumerate(CT):
                            inst = s.activation(gtr[:, off:off + w], ps[ti][0:128, 0:w],
                                                AF.Sigmoid)
                        p.ac.inc(inst)
                    ev_toks.append((p.ac, p.ac.v))
                    if C3LVL < 5:
                        return

                    @blk.sync
                    def _(s, c=c):
                        _w(s, p.ac)
                        inst = s.dma_start(out=io["gt16"][c * 128:(c + 1) * 128, :],
                                           in_=gtr[:])
                        p.st.inc(inst, 16)

            with nc.Block() as blk:
                emit_proj(p, blk, s3, tag="c3", Ms=Ms, nK=16, wap=wap_c3,
                          rhs=lambda k, off, w: xc16[:, k * T + off:k * T + off + w],
                          tiles=CT, evac=evac_c3, force_nsets=1)
            p.bar()

            with nc.Block() as blk:
                for c in range(NCH):
                    @blk.tensor
                    def _(t, c=c):
                        _w(t, p.dv)
                        inst = t.transpose(ps_tp[0:16, 0:128],
                                           bdbo[:, c * 16:(c + 1) * 16], ident[:])
                        p.pe.inc(inst)
                    pe_tp = p.pe.v

                    @blk.vector
                    def _(v, c=c, pe_tp=pe_tp):
                        _w(v, p.pe, pe_tp)
                        inst = v.tensor_copy(sstage[0:16, c * 128:(c + 1) * 128],
                                             ps_tp[0:16, 0:128])
                        p.dv.inc(inst)
            p.bar()

        cstk.close()

        if "6" not in parts:
            return
        if split:
            with nc.Block() as blk:
                @blk.sync
                def _(s):
                    inst = s.dma_start(out=io["statspart"][:, :], in_=sstage[:])
                    p.st.inc(inst, 16)
                    _w(s, p.st)
            p.bar()
        else:
            with contextlib.ExitStack() as s6:
                sp32 = s6.enter_context(nc.sbuf_tensor("c_sp", [16, STATS_ROWS], F32))
                pl_sb = s6.enter_context(nc.sbuf_tensor("c_pl", [STATS_ROWS, T], F32))
                ps_pl = s6.enter_context(nc.psum_tensor("c_ppl", [64, 2048], F32))
                with nc.Block() as blk:
                    @blk.sync
                    def _(s):
                        inst = s.dma_start(out=sp32[:], in_=io["selplace"][:, :])
                        p.sl.inc(inst, 16)
                    @blk.tensor
                    def _(t):
                        _w(t, p.sl)
                        for q4 in range(4):
                            inst = t.matmul(ps_pl[0:64, q4 * 512:(q4 + 1) * 512],
                                            sp32[0:16, 0:64],
                                            sstage[0:16, q4 * 512:(q4 + 1) * 512],
                                            start=True, stop=True)
                        p.pe.inc(inst)
                    pe_pl = p.pe.v

                    @blk.vector
                    def _(v):
                        _w(v, p.pe, pe_pl)
                        inst = v.tensor_copy(pl_sb[:], ps_pl[0:64, 0:2048])
                        p.dv.inc(inst)
                    dv_pl = p.dv.v

                    @blk.gpsimd
                    def _(g):
                        _w(g, p.dv, dv_pl)
                        inst = g.dma_start(out=io["statsbuf"][:, :], in_=pl_sb[:])
                        p.g2.inc(inst, 16)
                        _w(g, p.g2)
                        flat = io["statsbuf"].rearrange("a b -> (a b)")[None, :]
                        inst = g.collective_compute("AllReduce", ALU.add,
                                                    replica_groups=GROUPS,
                                                    ins=[flat], outs=[flat])
                        p.cc.inc(inst, 1)
                        _w(g, p.cc)
            p.bar()

        if "4" not in parts:
            return
        with contextlib.ExitStack() as s4:
            xn1r = s4.enter_context(nc.sbuf_tensor("c_x1r", [128, NCH * T], F16))
            hg = s4.enter_context(nc.sbuf_tensor("c_hg", [128, T], F16))
            huX = s4.enter_context(nc.sbuf_tensor("c_hu", [128, T], F16))
            h16r = s4.enter_context(nc.sbuf_tensor("c_h16", [128, T], F16))
            with nc.Block() as blk:
                @blk.sync
                def _(s):
                    _w(s, p.st)
                    inst = s.dma_start(
                        out=xn1r.rearrange("p (c n) -> p c n", c=NCH),
                        in_=dram_tiles(io["xn116s"], HALO, T))
                    p.sl.inc(inst, 16)
            p.bar()

            CT = [(512 * i, 512) for i in range(4)]
            Ms = []
            for i in range(ICH):
                Ms.append((i, 0))
                Ms.append((i, 1))

            def wap_f1(m):
                i, u = m
                return wslice(io["wu1T"] if u else io["wg1T"], i)

            def evac_f1(m, mi, ps, pe_tok, ev_toks):
                i, u = m
                if u == 0:
                    @blk.scalar
                    def _(s, ps=ps, pe_tok=pe_tok):
                        _w(s, p.pe, pe_tok)
                        _w(s, p.dv)
                        for ti, (off, w) in enumerate(CT):
                            inst = s.activation(hg[:, off:off + w], ps[ti][0:128, 0:w],
                                                AF.Silu)
                        p.ac.inc(inst)
                    ev_toks.append((p.ac, p.ac.v))
                else:
                    st_now = p.st.v

                    @blk.vector
                    def _(v, ps=ps, pe_tok=pe_tok, st_now=st_now):
                        _w(v, p.pe, pe_tok)
                        for ti, (off, w) in enumerate(CT):
                            inst = v.tensor_copy(huX[:, off:off + w], ps[ti][0:128, 0:w])
                        p.dv.inc(inst)
                    ev_toks.append((p.dv, p.dv.v))

                    @blk.vector
                    def _(v, i=i, st_now=st_now):
                        _w(v, p.ac)
                        _w(v, p.st, st_now)
                        inst = v.tensor_mul(h16r[:], hg[:], huX[:])
                        p.dv.inc(inst)
                    dv_h = p.dv.v

                    @blk.sync
                    def _(s, i=i, dv_h=dv_h):
                        _w(s, p.dv, dv_h)
                        inst = s.dma_start(out=io["h1"][i * 128:(i + 1) * 128, :],
                                           in_=h16r[:])
                        p.st.inc(inst, 16)

            with nc.Block() as blk:
                emit_proj(p, blk, s4, tag="c4", Ms=Ms, nK=16, wap=wap_f1,
                          rhs=lambda k, off, w: xn1r[:, k * T + off:k * T + off + w],
                          tiles=CT, evac=evac_f1, pe_pre=[(p.sl, p.sl.v)])
            p.bar()

        if "5" not in parts:
            return
        with contextlib.ExitStack() as s5:
            hmsb = s5.enter_context(nc.sbuf_tensor("c_hm", [128, ICH * 1024], F16))
            g1r = [s5.enter_context(nc.sbuf_tensor(f"c_g1{i}", [128, 1024], F16))
                   for i in range(2)]
            xnsl = s5.enter_context(nc.sbuf_tensor("c_xns", [128, 1024], F16))
            x1sl = s5.enter_context(nc.sbuf_tensor("c_x1s", [128, 1024], F32))
            tb16 = s5.enter_context(nc.sbuf_tensor("c_tb", [128, 1024], F16))
            p2r = s5.enter_context(nc.sbuf_tensor("c_p2", [128, 1024], F32))
            for si, sbase in enumerate((0, 1024)):
                with nc.Block() as blk:
                    @blk.sync
                    def _(s, sbase=sbase):
                        _w(s, p.st)
                        inst = s.dma_start(
                            out=hmsb.rearrange("p (c n) -> p c n", c=ICH),
                            in_=dram_tiles64(io["h1"], sbase, 1024))
                        p.sl.inc(inst, 16)
                p.bar()
                subs = [(0, 512), (512, 512)]

                def evac_d1(m, mi, ps, pe_tok, ev_toks, sbase=sbase):
                    @blk.scalar
                    def _(s, m=m, mi=mi, ps=ps, pe_tok=pe_tok):
                        _w(s, p.pe, pe_tok)
                        _w(s, p.st)
                        for ti, (off, w) in enumerate(subs):
                            inst = s.activation(g1r[mi % 2][:, off:off + w],
                                                ps[ti][0:128, 0:w], AF.Silu)
                        p.ac.inc(inst)
                    ev_toks.append((p.ac, p.ac.v))
                    ac_g = p.ac.v

                    @blk.sync
                    def _(s, m=m, mi=mi, sbase=sbase, ac_g=ac_g):
                        _w(s, p.dv)
                        _w(s, p.st)
                        inst = s.dma_start(out=xnsl[:],
                                           in_=io["xn116s"][m * 128:(m + 1) * 128,
                                                            HALO + sbase:HALO + sbase + 1024])
                        p.sl.inc(inst, 16)
                        inst = s.dma_start(out=x1sl[:],
                                           in_=io["x1s"][m * 128:(m + 1) * 128,
                                                         HALO + sbase:HALO + sbase + 1024])
                        p.sl.inc(inst, 16)
                        _w(s, p.ac, ac_g)
                        inst = s.dma_start(out=io["g116"][m * 128:(m + 1) * 128,
                                                          sbase:sbase + 1024],
                                           in_=g1r[mi % 2][:])
                        p.st.inc(inst, 16)
                    sl_x = p.sl.v

                    @blk.vector
                    def _(v, m=m, mi=mi, sbase=sbase, ac_g=ac_g, sl_x=sl_x):
                        _w(v, p.ac, ac_g)
                        _w(v, p.sl, sl_x)
                        v.tensor_mul(tb16[:], g1r[mi % 2][:], xnsl[:])
                        inst = v.tensor_add(p2r[:], x1sl[:], tb16[:])
                        p.dv.inc(inst)
                    dv_p2 = p.dv.v

                    @blk.sync
                    def _(s, m=m, sbase=sbase, dv_p2=dv_p2):
                        _w(s, p.dv, dv_p2)
                        inst = s.dma_start(out=io["P2s"][m * 128:(m + 1) * 128,
                                                         sbase:sbase + 1024],
                                           in_=p2r[:])
                        p.st.inc(inst, 16)

                with contextlib.ExitStack() as ss, nc.Block() as blk:
                    emit_proj(p, blk, ss, tag=f"c5{si}", Ms=list(range(16)), nK=ICH,
                              wap=lambda m: wslice(io["wd1T"], m),
                              rhs=lambda k, off, w: hmsb[:, k * 1024 + off:k * 1024 + off + w],
                              tiles=subs, evac=evac_d1, pe_pre=[(p.sl, p.sl.v)])
                p.bar()



def phase_d(p, io, split):
    nc = p.nc
    stats = io["statsall"] if split else io["statsbuf"]
    with contextlib.ExitStack() as stk:
        carry = stk.enter_context(nc.sbuf_tensor("d_cy", [128, 128], F32))
        ident = stk.enter_context(nc.sbuf_tensor("d_id", [128, 128], F32))

        with contextlib.ExitStack() as s0:
            bdT = s0.enter_context(nc.sbuf_tensor("d_bd", [32, T], F32))
            boT = s0.enter_context(nc.sbuf_tensor("d_bo", [32, T], F32))
            tri = s0.enter_context(nc.sbuf_tensor("d_tri", [32, 32], F32))
            tris = s0.enter_context(nc.sbuf_tensor("d_trs", [32, 32], F32))
            selc = s0.enter_context(nc.sbuf_tensor("d_sel", [32, 8], F32))
            o32 = s0.enter_context(nc.sbuf_tensor("d_o32", [1, 32], F32))
            cdsb = s0.enter_context(nc.sbuf_tensor("d_cd", [32, T], F32))
            en32 = s0.enter_context(nc.sbuf_tensor("d_en", [32, T], F32))
            seeds = s0.enter_context(nc.sbuf_tensor("d_sd", [32, T], F32))
            carr = s0.enter_context(nc.sbuf_tensor("d_ca", [32, T], F32))
            mcsb = s0.enter_context(nc.sbuf_tensor("d_mc", [8, T], F32))
            ps_a = s0.enter_context(nc.psum_tensor("d_pa", [128, 2048], F32))
            ps_b = s0.enter_context(nc.psum_tensor("d_pb", [128, 2048], F32))

            with nc.Block() as blk:
                @blk.sync
                def _(s):
                    _w(s, p.st)
                    _w(s, p.g2)
                    for r_ in range(4):
                        inst = s.dma_start(out=bdT[8 * r_:8 * (r_ + 1), :],
                                           in_=stats[16 * r_:16 * r_ + 8, :])
                        p.sl.inc(inst, 16)
                        inst = s.dma_start(out=boT[8 * r_:8 * (r_ + 1), :],
                                           in_=stats[16 * r_ + 8:16 * r_ + 16, :])
                        p.sl.inc(inst, 16)
                    for nm, t in (("tri32", tri), ("tri32s", tris),
                                  ("selcarry", selc), ("ident", ident)):
                        inst = s.dma_start(out=t[:], in_=io[nm][:, :])
                        p.sl.inc(inst, 16)
                @blk.vector
                def _(v):
                    inst = v.memset(o32[:], 1.0)
                    p.dv.inc(inst)

                @blk.tensor
                def _(t):
                    _w(t, p.sl)
                    for q in range(4):
                        inst = t.matmul(ps_a[0:32, q * 512:(q + 1) * 512], tri[:],
                                        bdT[:, q * 512:(q + 1) * 512],
                                        start=True, stop=True)
                    p.pe.inc(inst)
                pe_cd = p.pe.v

                @blk.vector
                def _(v):
                    _w(v, p.pe, pe_cd)
                    inst = v.tensor_scalar(cdsb[:], ps_a[0:32, 0:T], 0.0, -80.0,
                                           ALU.min, ALU.max)
                    p.dv.inc(inst)
                dv_cd = p.dv.v

                @blk.tensor
                def _(t):
                    _w(t, p.dv, dv_cd)
                    for q in range(4):
                        inst = t.matmul(ps_b[0:32, q * 512:(q + 1) * 512], o32[0:1, 0:32],
                                        cdsb[0:1, q * 512:(q + 1) * 512],
                                        start=True, stop=True)
                    p.pe.inc(inst)
                pe_st = p.pe.v

                @blk.vector
                def _(v):
                    _w(v, p.pe, pe_st)
                    v.tensor_sub(en32[:], cdsb[:], ps_b[0:32, 0:T])
                    inst = v.tensor_scalar(cdsb[:], en32[:], 0.0, -20.0,
                                           ALU.min, ALU.max)
                    p.dv.inc(inst)
                dv_ncd = p.dv.v

                @blk.scalar
                def _(s):
                    _w(s, p.dv, dv_ncd)
                    inst = s.activation(en32[:], cdsb[:], AF.Exp, scale=-1.0)
                    p.ac.inc(inst)
                ac_en = p.ac.v

                @blk.vector
                def _(v):
                    _w(v, p.ac, ac_en)
                    inst = v.tensor_mul(seeds[:], boT[:], en32[:])
                    p.dv.inc(inst)
                dv_seed = p.dv.v

                @blk.tensor
                def _(t):
                    _w(t, p.dv, dv_seed)
                    for q in range(4):
                        inst = t.matmul(ps_a[0:32, q * 512:(q + 1) * 512], tris[:],
                                        seeds[:, q * 512:(q + 1) * 512],
                                        start=True, stop=True)
                    p.pe.inc(inst)
                pe_cp = p.pe.v

                @blk.scalar
                def _(s):
                    _w(s, p.pe, pe_cp)
                    inst = s.activation(en32[:], cdsb[:], AF.Exp)
                    p.ac.inc(inst)
                ac_ep = p.ac.v

                @blk.vector
                def _(v):
                    _w(v, p.ac, ac_ep)
                    inst = v.tensor_mul(carr[:], ps_a[0:32, 0:T], en32[:])
                    p.dv.inc(inst)
                dv_carr = p.dv.v

                @blk.tensor
                def _(t):
                    _w(t, p.dv, dv_carr)
                    for q in range(4):
                        inst = t.matmul(ps_b[0:8, q * 512:(q + 1) * 512], selc[:],
                                        carr[:, q * 512:(q + 1) * 512],
                                        start=True, stop=True)
                    p.pe.inc(inst)
                pe_mc = p.pe.v

                @blk.vector
                def _(v):
                    _w(v, p.pe, pe_mc)
                    inst = v.tensor_copy(mcsb[:], ps_b[0:8, 0:T])
                    p.dv.inc(inst)
                dv_mc = p.dv.v

                for c in range(NCH):
                    @blk.tensor
                    def _(t, c=c):
                        _w(t, p.dv)
                        inst = t.transpose(ps_a[0:128, 0:8],
                                           mcsb[0:8, c * 128:(c + 1) * 128],
                                           ident[0:8, 0:8])
                        p.pe.inc(inst)
                    pe_t = p.pe.v

                    @blk.vector
                    def _(v, c=c):
                        _w(v, p.pe, pe_t)
                        inst = v.tensor_copy(carry[:, c * 8:(c + 1) * 8],
                                             ps_a[0:128, 0:8])
                        p.dv.inc(inst)
            p.bar()

        with contextlib.ExitStack() as s1:
            cot = s1.enter_context(nc.sbuf_tensor("d_co", [128, NCH * 512], F16))
            elt = s1.enter_context(nc.sbuf_tensor("d_el", [128, NCH * 512], F16))
            gtt = s1.enter_context(nc.sbuf_tensor("d_gt", [128, NCH * 512], F16))
            g1t = s1.enter_context(nc.sbuf_tensor("d_g1", [128, NCH * 512], F16))
            fin = s1.enter_context(nc.sbuf_tensor("d_fi", [128, NCH * 512], F16))
            vg = s1.enter_context(nc.sbuf_tensor("d_vg", [128, NCH * 512], F16))
            p2r = s1.enter_context(nc.sbuf_tensor("d_p2", [128, 512], F32))
            a16 = s1.enter_context(nc.sbuf_tensor("d_a16", [128, 512], F16))
            outr = s1.enter_context(nc.sbuf_tensor("d_or", [128, 512], F32))

            for ti in range(4):
                base = ti * 512
                with nc.Block() as blk:
                    @blk.sync
                    def _(s, base=base):
                        _w(s, p.st)
                        _w(s, p.g2)
                        for nm, t in (("co16", cot), ("eL16", elt),
                                      ("gt16", gtt), ("g116", g1t)):
                            inst = s.dma_start(
                                out=t.rearrange("p (c n) -> p c n", c=NCH),
                                in_=dram_tiles(io[nm], base, 512))
                            p.sl.inc(inst, 16)
                    sl_t = p.sl.v

                    @blk.vector
                    def _(v, ti=ti, base=base):
                        _w(v, p.sl, sl_t)
                        for c in range(NCH):
                            for sg in range(2):
                                gseg = ti * 2 + sg
                                inst = v.scalar_tensor_tensor(
                                    fin[:, c * 512 + sg * 256: c * 512 + (sg + 1) * 256],
                                    elt[:, c * 512 + sg * 256: c * 512 + (sg + 1) * 256],
                                    carry[:, c * 8 + gseg: c * 8 + gseg + 1],
                                    cot[:, c * 512 + sg * 256: c * 512 + (sg + 1) * 256],
                                    ALU.mult, ALU.add)
                        p.dv.inc(inst)
                p.bar()

                def evac_vm(m, mi, ps, pe_tok, ev_toks):
                    @blk.vector
                    def _(v, m=m, ps=ps, pe_tok=pe_tok):
                        _w(v, p.pe, pe_tok)
                        inst = v.tensor_mul(vg[:, m * 512:(m + 1) * 512],
                                            ps[0][0:128, 0:512],
                                            gtt[:, m * 512:(m + 1) * 512])
                        p.dv.inc(inst)
                    ev_toks.append((p.dv, p.dv.v))

                with contextlib.ExitStack() as ss, nc.Block() as blk:
                    emit_proj(p, blk, ss, tag=f"dvm{ti}", Ms=list(range(16)), nK=16,
                              wap=lambda m: wslice(io["BM"], m),
                              rhs=lambda k, off, w: fin[:, k * 512 + off:k * 512 + off + w],
                              tiles=[(0, 512)], evac=evac_vm)
                p.bar()

                def evac_op(m, mi, ps, pe_tok, ev_toks, base=base):
                    @blk.sync
                    def _(s, m=m, base=base):
                        _w(s, p.dv)
                        _w(s, p.st)
                        _w(s, p.g2)
                        inst = s.dma_start(out=p2r[:],
                                           in_=io["P2s"][m * 128:(m + 1) * 128,
                                                         base:base + 512])
                        p.sl.inc(inst, 16)
                    sl_p = p.sl.v

                    @blk.vector
                    def _(v, m=m, ps=ps, pe_tok=pe_tok, sl_p=sl_p):
                        _w(v, p.pe, pe_tok)
                        _w(v, p.sl, sl_p)
                        v.tensor_mul(a16[:], ps[0][0:128, 0:512],
                                     g1t[:, m * 512:(m + 1) * 512])
                        inst = v.tensor_add(outr[:], p2r[:], a16[:])
                        p.dv.inc(inst)
                    ev_toks.append((p.dv, p.dv.v))
                    dv_o = p.dv.v

                    @blk.sync
                    def _(s, m=m, base=base, dv_o=dv_o):
                        _w(s, p.dv, dv_o)
                        inst = s.dma_start(out=io["out"][m * 128:(m + 1) * 128,
                                                         base:base + 512],
                                           in_=outr[:])
                        p.st.inc(inst, 16)

                with contextlib.ExitStack() as ss, nc.Block() as blk:
                    emit_proj(p, blk, ss, tag=f"dop{ti}", Ms=list(range(16)), nK=16,
                              wap=lambda m: wslice(io["woutT"], m),
                              rhs=lambda k, off, w: vg[:, k * 512 + off:k * 512 + off + w],
                              tiles=[(0, 512)], evac=evac_op)
                p.bar()


def emit_final_wait(p):
    nc = p.nc
    with nc.Block() as blk:
        @blk.gpsimd
        def _(g):
            _w(g, p.st)
            _w(g, p.g2)
            _w(g, p.sl)
            _w(g, p.gl)



def _din(nc, name, shape, dtype=F32):
    return nc.dram_tensor(name, list(shape), dtype, kind="ExternalInput").ap()

def _dout(nc, name, shape, dtype=F32):
    return nc.dram_tensor(name, list(shape), dtype, kind="ExternalOutput").ap()

def _dint(nc, name, shape, dtype=F32, **kw):
    return nc.dram_tensor(name, list(shape), dtype, kind="Internal", **kw).ap()


def build_A(split):
    nc = bass.Bass("TRN2", target_bir_lowering=False, num_devices=8)
    p = P(nc)
    io = {}
    io["x_ext"] = _din(nc, "x_ext", [D, WE])
    io["wqkvT"] = _din(nc, "wqkvT", [D, 3 * D])
    io["wg0"] = _din(nc, "wg0", [D, IFF])
    io["wu0"] = _din(nc, "wu0", [D, IFF])
    io["wd0"] = _din(nc, "wd0", [IFF, D])
    io["hm0"] = _dint(nc, "hm0", [IFF, WE], F16)
    if split:
        io["q16"] = _dout(nc, "q16", [D, WE], F16)
        io["g016"] = _dout(nc, "g016", [D, WE], F16)
        io["vkbuf"] = _dout(nc, "vkpart", [VKN + 2048])
    else:
        io["q16"] = _dint(nc, "q16", [D, WE], F16)
        io["g016"] = _dint(nc, "g016", [D, WE], F16)
        io["vkbuf"] = _dint(nc, "vkbuf", [VKN + 2048])
    phase_a(p, io)
    if split:
        emit_final_wait(p)
    return nc, p, io


def build_BC(split, nc=None, p=None, io=None):
    if nc is None:
        nc = bass.Bass("TRN2", target_bir_lowering=False, num_devices=8)
        p = P(nc)
        io = {}
        io["x_ext"] = _din(nc, "x_ext", [D, WE])
        io["q16"] = _din(nc, "q16", [D, WE], F16)
        io["g016"] = _din(nc, "g016", [D, WE], F16)
        io["vkparts"] = _din(nc, "vkparts", [4, VKN + 2048])
        io["vkbuf"] = _dint(nc, "vkbuf", [VKN + 2048])
    io["wprojT"] = _din(nc, "wprojT", [D, D])
    io["bproj_r"] = _din(nc, "bproj_r", [128, 16])
    io["indden"] = _din(nc, "indden", [D, 64])
    io["bbc"] = _din(nc, "bbc", [64, D])
    io["winT"] = _din(nc, "winT", [D, 3 * D])
    io["convw_r"] = _din(nc, "convw_r", [128, 64])
    io["convb_r"] = _din(nc, "convb_r", [128, 16])
    io["nscale_r"] = _din(nc, "nscale_r", [128, 16])
    io["dtb_r"] = _din(nc, "dtb_r", [128, 16])
    io["cos_t"] = _din(nc, "cos_t", [64, T])
    io["sin_t"] = _din(nc, "sin_t", [64, T])
    io["wg1T"] = _din(nc, "wg1T", [D, IFF])
    io["wu1T"] = _din(nc, "wu1T", [D, IFF])
    io["wd1T"] = _din(nc, "wd1T", [IFF, D])
    io["ident"] = _din(nc, "ident", [128, 128])
    io["BD16"] = _dint(nc, "BD16", [16 * 128 * 128], F16)
    io["x1s"] = _dint(nc, "x1s", [D, WE])
    io["xn116s"] = _dint(nc, "xn116s", [D, WE], F16)
    io["h1"] = _dint(nc, "h1", [IFF, T], F16)
    if split:
        io["co16"] = _dout(nc, "co16", [D, T], F16)
        io["eL16"] = _dout(nc, "eL16", [D, T], F16)
        io["gt16"] = _dout(nc, "gt16", [D, T], F16)
        io["g116"] = _dout(nc, "g116", [D, T], F16)
        io["P2s"] = _dout(nc, "P2s", [D, T])
        io["statspart"] = _dout(nc, "statspart", [16, T])
    else:
        io["co16"] = _dint(nc, "co16", [D, T], F16)
        io["eL16"] = _dint(nc, "eL16", [D, T], F16)
        io["gt16"] = _dint(nc, "gt16", [D, T], F16)
        io["g116"] = _dint(nc, "g116", [D, T], F16)
        io["P2s"] = _dint(nc, "P2s", [D, T])
        io["selplace"] = _din(nc, "selplace", [16, STATS_ROWS])
        io["statsbuf"] = _dint(nc, "statsbuf", [STATS_ROWS, T])
    phase_b_full(p, io, split)
    phase_c(p, io, split)
    if split:
        emit_final_wait(p)
    return nc, p, io


def build_D(split, nc=None, p=None, io=None):
    if nc is None:
        nc = bass.Bass("TRN2", target_bir_lowering=False, num_devices=8)
        p = P(nc)
        io = {}
        io["statsall"] = _din(nc, "statsall", [STATS_ROWS, T])
        io["co16"] = _din(nc, "co16", [D, T], F16)
        io["eL16"] = _din(nc, "eL16", [D, T], F16)
        io["gt16"] = _din(nc, "gt16", [D, T], F16)
        io["g116"] = _din(nc, "g116", [D, T], F16)
        io["P2s"] = _din(nc, "P2s", [D, T])
        io["ident"] = _din(nc, "ident", [128, 128])
    io["BM"] = _din(nc, "BM", [D, D])
    io["woutT"] = _din(nc, "woutT", [D, D])
    io["tri32"] = _din(nc, "tri32", [32, 32])
    io["tri32s"] = _din(nc, "tri32s", [32, 32])
    io["selcarry"] = _din(nc, "selcarry", [32, 8])
    io["out"] = _dout(nc, "out", [D, T])
    phase_d(p, io, split)
    emit_final_wait(p)
    return nc, p, io


def build_fused():
    nc = bass.Bass("TRN2", target_bir_lowering=False, num_devices=8)
    p = P(nc)
    io = {}
    io["x_ext"] = _din(nc, "x_ext", [D, WE])
    io["wqkvT"] = _din(nc, "wqkvT", [D, 3 * D])
    io["wg0"] = _din(nc, "wg0", [D, IFF])
    io["wu0"] = _din(nc, "wu0", [D, IFF])
    io["wd0"] = _din(nc, "wd0", [IFF, D])
    io["hm0"] = _dint(nc, "hm0", [IFF, WE], F16)
    io["q16"] = _dint(nc, "q16", [D, WE], F16)
    io["g016"] = _dint(nc, "g016", [D, WE], F16)
    io["vkbuf"] = _dint(nc, "vkbuf", [VKN + 2048])
    phase_a(p, io)
    build_BC(False, nc, p, io)
    build_D(False, nc, p, io)
    return nc, p, io


def _core_inputs_A(sh, pc):
    return {"x_ext": pc["x_ext"], "wqkvT": sh["wqkvT"], "wg0": sh["wg0"],
            "wu0": sh["wu0"], "wd0": sh["wd0"]}

def _core_inputs_BC(sh, pc):
    return {"x_ext": pc["x_ext"], "wprojT": sh["wprojT"], "bproj_r": sh["bproj_r"],
            "indden": sh["indden"], "bbc": sh["bbc"], "winT": sh["winT"],
            "convw_r": sh["convw_r"], "convb_r": sh["convb_r"],
            "nscale_r": sh["nscale_r"], "dtb_r": sh["dtb_r"],
            "cos_t": pc["cos_t"], "sin_t": pc["sin_t"], "wg1T": sh["wg1T"],
            "wu1T": sh["wu1T"], "wd1T": sh["wd1T"], "ident": sh["ident"]}

def _core_inputs_D(sh, pc):
    return {"BM": sh["BM"], "woutT": sh["woutT"], "tri32": sh["tri32"],
            "tri32s": sh["tri32s"], "selcarry": pc["selcarry"],
            "ident": sh["ident"]}


def kernel(**inputs):
    sh, cores = host_prep(inputs)
    y = np.empty((2, 8192, 2048), np.float32)

    if MODE == "fused":
        nc, p, io = build_fused()
        in_maps = []
        for c in range(8):
            m = {}
            m.update(_core_inputs_A(sh, cores[c]))
            m.update(_core_inputs_BC(sh, cores[c]))
            m.update(_core_inputs_D(sh, cores[c]))
            m["selplace"] = cores[c]["selplace"]
            in_maps.append(m)
        res = run_bass_kernel_spmd(nc, in_maps, core_ids=list(range(8)))
        for c in range(8):
            b, r = c // 4, c % 4
            y[b, 2048 * r:2048 * (r + 1), :] = res.results[c]["out"].T
        return y

    ncA, pA, _ = build_A(True)
    resA = run_bass_kernel_spmd(ncA, [_core_inputs_A(sh, cores[c]) for c in range(8)],
                                core_ids=list(range(8)))
    ncBC, pBC, _ = build_BC(True)
    in_maps = []
    for c in range(8):
        g = (c // 4) * 4
        m = _core_inputs_BC(sh, cores[c])
        m["q16"] = resA.results[c]["q16"]
        m["g016"] = resA.results[c]["g016"]
        m["vkparts"] = np.stack([resA.results[g + i]["vkpart"] for i in range(4)])
        in_maps.append(m)
    resBC = run_bass_kernel_spmd(ncBC, in_maps, core_ids=list(range(8)))

    ncD, pD, _ = build_D(True)
    in_maps = []
    for c in range(8):
        g = (c // 4) * 4
        m = _core_inputs_D(sh, cores[c])
        m["statsall"] = np.concatenate(
            [resBC.results[g + i]["statspart"] for i in range(4)], axis=0)
        for nm in ("co16", "eL16", "gt16", "g116", "P2s"):
            m[nm] = resBC.results[c][nm]
        in_maps.append(m)
    resD = run_bass_kernel_spmd(ncD, in_maps, core_ids=list(range(8)))
    for c in range(8):
        b, r = c // 4, c % 4
        y[b, 2048 * r:2048 * (r + 1), :] = resD.results[c]["out"].T
    return y
